# version 02
# baseline (speedup 1.0000x reference)
import math
import numpy as np
import ml_dtypes
import concourse.bass as bass
import concourse.mybir as mybir
from concourse.bass_utils import run_bass_kernel_spmd

F32 = mybir.dt.float32
BF16 = mybir.dt.bfloat16
AF = mybir.ActivationFunctionType
ALU = mybir.AluOpType
AX = mybir.AxisListType

NCORES = 8
B, S, D = 16, 2048, 2048
H, KVH, DH = 16, 4, 128
GQ = H // KVH
E, I_FF, IS_FF = 16, 1408, 2816
TOPK = 6
SCALE = DH ** -0.5
LN_S = math.log(SCALE)
EPS_RMS = 1e-8
EPS_LN = 1e-5
NEG = -1.0e38

SEQ_PER_CORE = 2
PANEL = 512
NPANEL = S // PANEL
HALF = 1024
NHALF = S // HALF
KD = D // 128
NPAIR = I_FF // 128
NPAIR_S = IS_FF // 128


class Sem:
    __slots__ = ("h", "n")
    def __init__(self, h):
        self.h = h
        self.n = 0


class Prog:
    ENGS = ("sync", "tensor", "vector", "scalar", "gpsimd")

    def __init__(self, nc):
        self.nc = nc
        self.q = {k: [] for k in self.ENGS}
        self.sems = []

    def sem(self, name):
        s = Sem(self.nc.alloc_semaphore(name))
        self.sems.append(s)
        return s

    def wait(self, eng, sem, v=None):
        vv = sem.n if v is None else v
        if vv > 0:
            self.q[eng].append(lambda e, s=sem.h, x=vv: e.wait_ge(s, x))

    def op(self, eng, fn, inc=None, k=1, tag=None):
        if inc is not None:
            inc.n += k
            if tag:
                self.q[eng].append(lambda e, f=fn, s=inc.h, kk=k, t=tag:
                                   f(e).then_inc(s, kk).annotate(t))
            else:
                self.q[eng].append(lambda e, f=fn, s=inc.h, kk=k: f(e).then_inc(s, kk))
        else:
            if tag:
                self.q[eng].append(lambda e, f=fn, t=tag: f(e).annotate(t))
            else:
                self.q[eng].append(lambda e, f=fn: f(e))

    def dma(self, eng, out, in_, inc, tag=None, **kw):
        inc.n += 16
        if tag:
            self.q[eng].append(
                lambda e, o=out, i=in_, s=inc.h, kws=kw, t=tag:
                e.dma_start(o, i, **kws).then_inc(s, 16).annotate(t))
        else:
            self.q[eng].append(
                lambda e, o=out, i=in_, s=inc.h, kws=kw: e.dma_start(o, i, **kws).then_inc(s, 16))

    def barrier(self):
        for eng in self.ENGS:
            for s in self.sems:
                self.wait(eng, s)


def build_core(cfg=None):
    cfg = cfg or {}
    seqs = cfg.get("seqs", list(range(SEQ_PER_CORE)))
    panels = cfg.get("panels", list(range(NPANEL)))
    halves = cfg.get("halves", list(range(NHALF)))
    heads = cfg.get("heads", list(range(H)))
    experts = cfg.get("experts", [16, 17] + list(range(E)))

    nc = bass.Bass("TRN2", target_bir_lowering=False)

    x_in = nc.dram_tensor("x", [SEQ_PER_CORE, S, D], F32, kind="ExternalInput")
    xtp = nc.dram_tensor("xtp", [SEQ_PER_CORE, NPANEL, 128, KD * PANEL], F32, kind="ExternalInput")
    wqp = nc.dram_tensor("wqp", [H, 128, KD * 128], F32, kind="ExternalInput")
    wkp = nc.dram_tensor("wkp", [KVH, 128, KD * 128], F32, kind="ExternalInput")
    wvp = nc.dram_tensor("wvp", [KD, 128, KVH * DH], F32, kind="ExternalInput")
    wop = nc.dram_tensor("wop", [4, 128, KD * 512], F32, kind="ExternalInput")
    w1p = nc.dram_tensor("w1p", [E, NPAIR, 128, KD * 256], F32, kind="ExternalInput")
    w2p = nc.dram_tensor("w2p", [E, 4, 128, NPAIR * 512], F32, kind="ExternalInput")
    sw1p = nc.dram_tensor("sw1p", [NPAIR_S, 128, KD * 256], F32, kind="ExternalInput")
    sw2p = nc.dram_tensor("sw2p", [2, 4, 128, NPAIR * 512], F32, kind="ExternalInput")
    gwp = nc.dram_tensor("gwp", [128, KD * E], F32, kind="ExternalInput")
    qg = nc.dram_tensor("qg", [128, 1], F32, kind="ExternalInput")
    kg = nc.dram_tensor("kg", [128, 1], F32, kind="ExternalInput")

    y = nc.dram_tensor("y", [SEQ_PER_CORE, S, D], F32, kind="ExternalOutput")

    xtp_bf = nc.dram_tensor("xtp_bf", [SEQ_PER_CORE, NPANEL, 128, KD * PANEL], BF16)
    wqp_bf = nc.dram_tensor("wqp_bf", [H, 128, KD * 128], BF16)
    wkp_bf = nc.dram_tensor("wkp_bf", [KVH, 128, KD * 128], BF16)
    wvp_bf = nc.dram_tensor("wvp_bf", [KD, 128, KVH * DH], BF16)
    wop_bf = nc.dram_tensor("wop_bf", [4, 128, KD * 512], BF16)
    x1f_scr = nc.dram_tensor("x1f_scr", [SEQ_PER_CORE, S, D], F32,
                             kind="ExternalOutput" if cfg.get("tap") else "Internal")
    x1bf_scr = nc.dram_tensor("x1bf_scr", [SEQ_PER_CORE, S, D], BF16)
    tap2 = bool(cfg.get("tap2"))
    if tap2:
        t_h2 = nc.dram_tensor("t_h2", [SEQ_PER_CORE, S, D], F32, kind="ExternalOutput")
        t_cmb = nc.dram_tensor("t_cmb", [SEQ_PER_CORE, NHALF, 128, 128], F32, kind="ExternalOutput")

    ident_dr = nc.inline_tensor(np.eye(128, dtype=ml_dtypes.bfloat16), name="identconst")

    ident = nc.alloc_sbuf_tensor("ident", [128, 128], BF16)
    gw_sb = nc.alloc_sbuf_tensor("gw_sb", [128, KD * E], BF16)
    mask_sb = nc.alloc_sbuf_tensor("mask_sb", [128, 128], F32)
    ones_col = nc.alloc_sbuf_tensor("ones_col", [128, 1], BF16)
    ones_row = nc.alloc_sbuf_tensor("ones_row", [1, 128], BF16)
    gamq = nc.alloc_sbuf_tensor("gamq", [128, 1], F32)
    gamk = nc.alloc_sbuf_tensor("gamk", [128, 1], F32)
    ceps_rms = nc.alloc_sbuf_tensor("ceps_rms", [1, 1], F32)
    ceps_ln = nc.alloc_sbuf_tensor("ceps_ln", [128, 1], F32)
    cln_s = nc.alloc_sbuf_tensor("cln_s", [1, 1], F32)
    rrow = nc.alloc_sbuf_tensor("rrow", [1, 1024], F32)
    rrow_bf = nc.alloc_sbuf_tensor("rrow_bf", [1, 1024], BF16)
    sqv_sb = nc.alloc_sbuf_tensor("sqv_sb", [128, 512], BF16)

    AB = nc.alloc_sbuf_tensor("AB", [128, 61440], BF16)
    AFA = nc.alloc_sbuf_tensor("AFA", [128, 19584], F32)

    def ab(off, n):
        return AB[:, off:off + n]

    def af(off, n):
        return AFA[:, off:off + n]

    O_XTA = 0
    O_KN = 16384
    O_VT = 24576
    O_WK = 32768
    O_WV = 40960
    O_WQ = 32768
    O_QN = 36864
    O_A = 37888
    O_A2 = 41984
    O_AT = 44032
    O_WO = 36864
    O_X1B = 45056
    O_AO = 53248
    C_X1T = 0
    C_ACT = 16384
    C_W1 = 38912
    C_W2 = 47104
    C_SG = 58368
    C_DISC = 59392

    F_H1 = 0
    F_XT = 8192
    F_KRAW = 12288
    F_HSQ = 13312
    F_ROWS = 18560
    C_MOE = 0
    C_OUT = 16384
    C_CMB = 18432

    P = Prog(nc)
    sCast = P.sem("sCast")
    sW1 = P.sem("sW1")
    sW2 = P.sem("sW2")
    sLd = P.sem("sLd")
    sSt = P.sem("sSt")
    sOut = P.sem("sOut")
    sPE = P.sem("sPE")
    sDV = P.sem("sDV")
    sAC = P.sem("sAC")
    sPO = P.sem("sPO")

    P.dma("sync", ident[:, :], ident_dr[:, :], sLd)
    P.dma("sync", gamq[:, :], qg[:, :], sLd, allow_slow_non_contiguous=True)
    P.dma("sync", gamk[:, :], kg[:, :], sLd, allow_slow_non_contiguous=True)
    consts_ld = sLd.n
    P.op("vector", lambda e: e.memset(ones_col[:, :], 1.0), inc=sDV)
    P.op("vector", lambda e: e.memset(ones_row[:, :], 1.0), inc=sDV)
    P.op("vector", lambda e: e.memset(ceps_rms[:, :], EPS_RMS), inc=sDV)
    P.op("vector", lambda e: e.memset(ceps_ln[:, :], EPS_LN), inc=sDV)
    P.op("vector", lambda e: e.memset(cln_s[:, :], LN_S), inc=sDV)
    consts_dv = sDV.n
    P.op("gpsimd", lambda e: e.memset(mask_sb[:, :], 0.0), inc=sPO)
    P.op("gpsimd", lambda e: e.affine_select(mask_sb[:, :], mask_sb[:, :], [[-1, 128]],
                                             ALU.is_ge, NEG, base=0, channel_multiplier=1), inc=sPO)
    mask_po = sPO.n
    P.dma("gpsimd", gw_sb[:, :], gwp[:, :], sCast)
    gw_cast = sCast.n
    for s in seqs:
        for p in panels:
            P.dma("gpsimd", xtp_bf[s, p], xtp[s, p], sCast)
    P.dma("gpsimd", wkp_bf[:, :, :], wkp[:, :, :], sCast)
    P.dma("gpsimd", wvp_bf[:, :, :], wvp[:, :, :], sCast)
    cast_attn_kv = sCast.n
    P.dma("gpsimd", wqp_bf[:, :, :], wqp[:, :, :], sCast)
    P.dma("gpsimd", wop_bf[:, :, :], wop[:, :, :], sCast)
    cast_attn_all = sCast.n

    PSF = nc.alloc_psum_tensor("PSF", [128, 4096], F32)

    def bankf(i, w=512):
        return PSF[:, i * 512: i * 512 + w]

    def bank_bf(i):
        return PSF[:, i * 512: i * 512 + 512].bitcast(BF16)

    def mm(out, lhsT, rhs, start, stop, inc=None, tag=None):
        P.op("tensor",
             lambda e, o=out, l=lhsT, r=rhs, st=start, sp=stop:
             e.matmul(o, l, r, start=st, stop=sp), inc=inc, tag=tag)

    def rstd_chain(row_psum, n_valid, scale_inv, fold_s, eps_cell):
        P.wait("scalar", sPE)
        P.op("scalar", lambda e, rp=row_psum, n=n_valid, si=scale_inv, ec=eps_cell:
             e.activation(rrow[0:1, 0:n], rp, AF.Ln, bias=ec[0:1, 0:1], scale=si), inc=sAC)
        if fold_s:
            P.op("scalar", lambda e, n=n_valid: e.activation(
                rrow[0:1, 0:n], rrow[0:1, 0:n], AF.Exp, bias=cln_s[0:1, 0:1], scale=-0.5), inc=sAC)
        else:
            P.op("scalar", lambda e, n=n_valid: e.activation(
                rrow[0:1, 0:n], rrow[0:1, 0:n], AF.Exp, bias=0.0, scale=-0.5), inc=sAC)
        P.wait("vector", sAC)
        P.wait("vector", sPE)
        P.op("vector", lambda e, n=n_valid: e.tensor_copy(rrow_bf[0:1, 0:n], rrow[0:1, 0:n]), inc=sDV)

    for s in seqs:
        P.barrier()
        P.wait("sync", sCast, cast_attn_kv)
        wk_sb = ab(O_WK, 4 * 2048)
        wv_sb = ab(O_WV, KD * 512)
        P.dma("sync", wk_sb.rearrange("p (h n) -> p h n", h=KVH),
              wkp_bf[:, :, :].rearrange("h p n -> p h n"), sLd)
        P.dma("sync", wv_sb.rearrange("p (k n) -> p k n", k=KD),
              wvp_bf[:, :, :].rearrange("k p n -> p k n"), sLd)

        kn_all = ab(O_KN, KVH * 2048)
        vt_all = ab(O_VT, KD * 512)

        for p in panels:
            xta = ab(O_XTA + (p % 2) * 8192, KD * PANEL)
            P.wait("sync", sCast, cast_attn_kv)
            P.wait("sync", sPE)
            P.dma("sync", xta, xtp_bf[s, p], sLd)
            xta_ld = sLd.n

            for kv in range(KVH):
                bk = kv % 2
                P.wait("tensor", sLd, xta_ld)
                P.wait("tensor", sAC)
                for k in range(KD):
                    mm(bankf(bk), wk_sb[:, kv * 2048 + k * 128: kv * 2048 + (k + 1) * 128],
                       xta[:, k * PANEL:(k + 1) * PANEL], k == 0, k == KD - 1,
                       inc=sPE if k == KD - 1 else None, tag="kproj")
                pe_kp = sPE.n
                P.wait("scalar", sPE, pe_kp)
                P.op("scalar", lambda e, b=bk: e.activation(af(F_KRAW, 512), bankf(b), AF.Copy,
                                                            bias=0.0, scale=1.0), inc=sAC)
                P.wait("vector", sAC)
                P.wait("vector", sPE)
                P.op("vector", lambda e: e.tensor_mul(sqv_sb[:, :], af(F_KRAW, 512), af(F_KRAW, 512)), inc=sDV)
                P.wait("tensor", sDV)
                mm(PSF[0:1, 4 * 512:4 * 512 + 512], ones_col[:, :], sqv_sb[:, :], True, True, inc=sPE, tag="knorm")
                rstd_chain(PSF[0:1, 4 * 512:4 * 512 + 512], 512, 1.0 / 128.0, False, ceps_rms)
                P.wait("tensor", sDV)
                mm(bankf(5), ones_row[:, :], rrow_bf[0:1, 0:512], True, True, inc=sPE, tag="knorm")
                P.wait("vector", sPE)
                P.op("vector", lambda e, kv_=kv, p_=p: e.scalar_tensor_tensor(
                    kn_all[:, kv_ * 2048 + p_ * PANEL: kv_ * 2048 + (p_ + 1) * PANEL],
                    af(F_KRAW, 512), gamk[:, :], bankf(5), ALU.mult, ALU.mult), inc=sDV)

            for ts in range(4):
                bk = 2 + ts % 2
                P.wait("tensor", sDV)
                for k in range(KD):
                    mm(bankf(bk), xta[:, k * PANEL + ts * 128: k * PANEL + ts * 128 + 128],
                       wv_sb[:, k * 512:(k + 1) * 512], k == 0, k == KD - 1,
                       inc=sPE if k == KD - 1 else None, tag="vproj")
                P.wait("vector", sPE)
                tc = p * 4 + ts
                P.op("vector", lambda e, b=bk, t=tc: e.tensor_copy(
                    vt_all[:, t * 512:(t + 1) * 512], bankf(b)), inc=sDV)

        kn_ready_dv = sDV.n

        P.barrier()
        for p in panels:
            xta = ab(O_XTA + (p % 2) * 8192, KD * PANEL)
            P.wait("sync", sPE)
            P.wait("sync", sSt)
            P.dma("sync", xta, xtp_bf[s, p], sLd)
            xta_ld = sLd.n
            ke_panel = (p + 1) * PANEL
            P.wait("vector", sSt)
            P.wait("scalar", sSt)

            qn_done_dv = [0, 0]
            av_done_ac = [0, 0]
            exp_done_ac = 0
            at_done_dv = 0

            for ih, h in enumerate(heads):
                par = ih % 2
                kv = h // GQ
                wq_sb = ab(O_WQ + par * 2048, 2048)
                P.wait("sync", sCast, cast_attn_all)
                P.wait("sync", sPE)
                P.dma("sync", wq_sb, wqp_bf[h], sLd)
                wq_ld = sLd.n

                qb = 4 + par
                P.wait("tensor", sLd, wq_ld)
                P.wait("tensor", sLd, xta_ld)
                P.wait("tensor", sAC, av_done_ac[par])
                P.wait("tensor", sDV, qn_done_dv[par])
                for k in range(KD):
                    mm(bankf(qb), wq_sb[:, k * 128:(k + 1) * 128],
                       xta[:, k * PANEL:(k + 1) * PANEL], k == 0, k == KD - 1,
                       inc=sPE if k == KD - 1 else None, tag="qproj")
                pe_qp = sPE.n
                P.wait("scalar", sPE, pe_qp)
                P.op("scalar", lambda e, b=qb: e.activation(af(F_KRAW, 512), bankf(b), AF.Copy,
                                                            bias=0.0, scale=1.0), inc=sAC)
                P.wait("vector", sAC)
                P.wait("vector", sPE)
                P.op("vector", lambda e: e.tensor_mul(sqv_sb[:, :], af(F_KRAW, 512), af(F_KRAW, 512)), inc=sDV)
                P.wait("tensor", sDV)
                mm(PSF[0:1, qb * 512: qb * 512 + 512], ones_col[:, :], sqv_sb[:, :], True, True, inc=sPE, tag="qnorm")
                rstd_chain(PSF[0:1, qb * 512: qb * 512 + 512], 512, 1.0 / 128.0, True, ceps_rms)
                P.wait("tensor", sDV)
                mm(bankf(qb), ones_row[:, :], rrow_bf[0:1, 0:512], True, True, inc=sPE, tag="qnorm")
                qn = ab(O_QN + par * 512, 512)
                P.wait("vector", sPE)
                P.op("vector", lambda e, b=qb, q=qn: e.scalar_tensor_tensor(
                    q, af(F_KRAW, 512), gamq[:, :], bankf(b), ALU.mult, ALU.mult), inc=sDV)
                qn_done_dv[par] = sDV.n

                P.wait("tensor", sDV, qn_done_dv[par])
                P.wait("tensor", sDV, kn_ready_dv)
                P.wait("vector", sPO, mask_po)
                for qi in range(4):
                    abuf = ab(O_A + (qi % 2) * 2048, 2048)
                    ke = p * PANEL + (qi + 1) * 128
                    P.wait("tensor", sAC, exp_done_ac)
                    nfull, rem = divmod(ke, 512)
                    nmm = nfull + (1 if rem else 0)
                    cnt = 0
                    for kt in range(nfull):
                        cnt += 1
                        mm(PSF[:, kt * 512:(kt + 1) * 512],
                           qn[:, qi * 128:(qi + 1) * 128],
                           kn_all[:, kv * 2048 + kt * 512: kv * 2048 + (kt + 1) * 512],
                           True, True, inc=sPE if cnt == nmm else None, tag="score")
                    if rem:
                        mm(PSF[:, nfull * 512: nfull * 512 + rem],
                           qn[:, qi * 128:(qi + 1) * 128],
                           kn_all[:, kv * 2048 + nfull * 512: kv * 2048 + nfull * 512 + rem],
                           True, True, inc=sPE, tag="score")
                    pe_sc = sPE.n
                    P.wait("vector", sPE, pe_sc)
                    P.op("vector", lambda e, k0=ke - 128: e.tensor_add(
                        PSF[:, k0:k0 + 128], PSF[:, k0:k0 + 128], mask_sb[:, :]), inc=sDV)
                    dv_msk = sDV.n
                    P.wait("vector", sDV, dv_msk)
                    P.op("vector", lambda e, k1=ke: e.reduce_max(
                        af(F_ROWS, 1), PSF[:, 0:k1], AX.X, negate=True), inc=sDV)
                    dv_max = sDV.n
                    P.wait("scalar", sDV, dv_max)
                    P.op("scalar", lambda e, k1=ke, ab_=abuf: e.activation(
                        ab_[:, 0:k1], PSF[:, 0:k1], AF.Exp, bias=af(F_ROWS, 1), scale=1.0,
                        accum_out=af(F_ROWS + 1, 1)), inc=sAC)
                    exp_done_ac = sAC.n
                    P.wait("vector", sAC, exp_done_ac)
                    P.op("vector", lambda e: e.reciprocal(af(F_ROWS + 2, 1), af(F_ROWS + 1, 1)), inc=sDV)
                    dv_rc = sDV.n
                    a2 = ab(O_A2, 2048)
                    P.wait("scalar", sDV, dv_rc)
                    P.wait("scalar", sPE)
                    P.op("scalar", lambda e, k1=ke, ab_=abuf, a2_=a2: e.activation(
                        a2_[:, 0:k1], ab_[:, 0:k1], AF.Copy, bias=0.0, scale=af(F_ROWS + 2, 1)), inc=sAC)
                    resc_ac = sAC.n
                    nkc = ke // 128
                    P.wait("tensor", sAC, resc_ac)
                    at_v = ab(O_AT, 8192).rearrange("p (kc q) -> p kc q", q=512)
                    done = 0
                    while done < nkc:
                        nb = min(8, nkc - done)
                        tb = 6 + (done // 8) % 2
                        P.wait("tensor", sDV, at_done_dv)
                        for kk in range(nb):
                            kc = done + kk
                            P.op("tensor", lambda e, a2_=a2, kc_=kc, tb_=tb, kk_=kk: e.transpose(
                                bank_bf(tb_)[:, kk_ * 128:(kk_ + 1) * 128],
                                a2_[:, kc_ * 128:(kc_ + 1) * 128], ident[:, :]),
                                inc=sPE if kk == nb - 1 else None, tag="tp")
                        pe_tp = sPE.n
                        P.wait("vector", sPE, pe_tp)
                        P.op("vector", lambda e, tb_=tb, d0=done, nb_=nb, qi_=qi: e.tensor_copy(
                            at_v[:, d0:d0 + nb_, qi_ * 128:(qi_ + 1) * 128],
                            bank_bf(tb_).rearrange("p (kk q) -> p kk q", q=128)[:, 0:nb_, :]), inc=sDV)
                        at_done_dv = sDV.n
                        done += nb

                avb = 5 - par
                P.wait("tensor", sDV, at_done_dv)
                if ih >= 1:
                    P.wait("tensor", sDV, qn_done_dv[1 - par])
                nkc_p = ke_panel // 128
                for kc in range(nkc_p):
                    qs = 128 * max(0, kc - 4 * p)
                    mm(PSF[:, avb * 512 + qs: avb * 512 + 512],
                       vt_all[:, kc * 512 + kv * 128: kc * 512 + (kv + 1) * 128],
                       ab(O_AT, 8192)[:, kc * 512 + qs: kc * 512 + 512],
                       kc == 0, kc == nkc_p - 1, inc=sPE if kc == nkc_p - 1 else None, tag="av")
                pe_av = sPE.n
                P.wait("scalar", sPE, pe_av)
                P.op("scalar", lambda e, b=avb, h_=h: e.activation(
                    ab(O_AO + h_ * 512, 512), bankf(b), AF.Copy, bias=0.0, scale=1.0), inc=sAC)
                av_done_ac[1 - par] = sAC.n

            heads_done_ac = sAC.n

            o_bank_ac = [0, 0]
            for dt in range(4):
                wo_sb = ab(O_WO + (dt % 2) * 8192, 8192)
                P.wait("sync", sAC, heads_done_ac)
                P.wait("sync", sSt)
                if dt >= 2:
                    P.wait("sync", sPE)
                P.dma("sync", wo_sb, wop_bf[dt], sLd)
                wo_ld = sLd.n
                for tcl in range(4):
                    ob = 4 + tcl % 2
                    P.wait("tensor", sLd, wo_ld)
                    P.wait("tensor", sAC, o_bank_ac[tcl % 2])
                    if dt == 0:
                        P.wait("tensor", sAC, heads_done_ac)
                    for fh in range(H):
                        mm(bankf(ob), ab(O_AO + fh * 512 + tcl * 128, 128),
                           wo_sb[:, fh * 512:(fh + 1) * 512], fh == 0, fh == H - 1,
                           inc=sPE if fh == H - 1 else None, tag="oproj")
                    pe_o = sPE.n
                    P.wait("scalar", sPE, pe_o)
                    P.op("scalar", lambda e, b=ob, t=tcl, d=dt: e.activation(
                        af(F_H1 + t * 2048 + d * 512, 512), bankf(b), AF.Copy,
                        bias=0.0, scale=1.0), inc=sAC)
                    o_bank_ac[tcl % 2] = sAC.n
            o_done_ac = sAC.n

            for tcl in range(4):
                tc = p * 4 + tcl
                xt_f = af(F_XT + (tcl % 2) * 2048, 2048)
                P.wait("sync", sDV)
                P.dma("sync", xt_f, x_in[s, tc * 128:(tc + 1) * 128, :], sLd)
                xt_ld = sLd.n
                h1v = af(F_H1 + tcl * 2048, 2048)
                P.wait("vector", sLd, xt_ld)
                P.wait("vector", sAC, o_done_ac)
                P.op("vector", lambda e, xv=xt_f, hv=h1v: e.scalar_tensor_tensor(
                    hv, xv, 2.0, hv, ALU.mult, ALU.add), inc=sDV)
                P.op("vector", lambda e, hv=h1v: e.reduce_sum(af(F_ROWS + 4, 1), hv, AX.X), inc=sDV)
                dv_red = sDV.n
                P.wait("scalar", sDV, dv_red)
                P.op("scalar", lambda e, hv=h1v: e.activation(
                    af(F_HSQ, 2048), hv, AF.Square, bias=0.0, scale=1.0,
                    accum_out=af(F_ROWS + 5, 1)), inc=sAC)
                P.wait("vector", sDV, dv_red)
                P.op("vector", lambda e: e.tensor_scalar(af(F_ROWS + 6, 1), af(F_ROWS + 4, 1),
                                                         1.0 / D, None, ALU.mult), inc=sDV)
                dv_m = sDV.n
                P.wait("scalar", sDV, dv_m)
                P.op("scalar", lambda e: e.activation(af(F_ROWS + 7, 1), af(F_ROWS + 6, 1),
                                                      AF.Square, bias=0.0, scale=1.0), inc=sAC)
                msq_ac = sAC.n
                P.wait("vector", sAC, msq_ac)
                P.op("vector", lambda e: e.scalar_tensor_tensor(
                    af(F_ROWS + 8, 1), af(F_ROWS + 5, 1), 1.0 / D, af(F_ROWS + 7, 1),
                    ALU.mult, ALU.subtract), inc=sDV)
                dv_var = sDV.n
                P.wait("scalar", sDV, dv_var)
                P.op("scalar", lambda e: e.activation(af(F_ROWS + 9, 1), af(F_ROWS + 8, 1),
                                                      AF.Ln, bias=ceps_ln[:, 0:1], scale=1.0), inc=sAC)
                P.op("scalar", lambda e: e.activation(af(F_ROWS + 9, 1), af(F_ROWS + 9, 1),
                                                      AF.Exp, bias=0.0, scale=-0.5), inc=sAC)
                rstd_ac = sAC.n
                P.wait("vector", sAC, rstd_ac)
                P.op("vector", lambda e, hv=h1v: e.tensor_scalar(
                    hv, hv, af(F_ROWS + 6, 1), af(F_ROWS + 9, 1), ALU.subtract, ALU.mult), inc=sDV)
                x1bf = ab(O_X1B, 2048)
                P.wait("vector", sSt)
                P.op("vector", lambda e, hv=h1v, xb=x1bf: e.tensor_copy(xb, hv), inc=sDV)
                x1_dv = sDV.n
                P.wait("sync", sDV, x1_dv)
                P.dma("sync", x1f_scr[s, tc * 128:(tc + 1) * 128, :], h1v, sSt)
                P.dma("sync", x1bf_scr[s, tc * 128:(tc + 1) * 128, :], x1bf, sSt)
        x1_stores = sSt.n

        P.barrier()
        for hf in halves:
            x1t = ab(C_X1T, KD * HALF)
            P.wait("sync", sSt, x1_stores)
            P.wait("sync", sPE)
            for k in range(KD):
                P.dma("sync", x1t[:, k * HALF:(k + 1) * HALF],
                      x1bf_scr[s, hf * HALF:(hf + 1) * HALF, k * 128:(k + 1) * 128],
                      sLd, transpose=True)
            x1t_ld = sLd.n
            P.wait("sync", sDV)
            for tc in range(8):
                P.dma("sync", af(C_MOE + tc * 2048, 2048),
                      x1f_scr[s, hf * HALF + tc * 128: hf * HALF + (tc + 1) * 128, :], sLd)
            seed_ld = sLd.n

            cmb = af(C_CMB, 128)
            P.wait("tensor", sLd, x1t_ld)
            P.wait("tensor", sCast, gw_cast)
            for tc in range(8):
                P.wait("tensor", sDV)
                for k in range(KD):
                    mm(PSF[:, 6 * 512: 6 * 512 + 16],
                       x1t[:, k * HALF + tc * 128: k * HALF + (tc + 1) * 128],
                       gw_sb[:, k * 16:(k + 1) * 16], k == 0, k == KD - 1,
                       inc=sPE if k == KD - 1 else None, tag="router")
                pe_lg = sPE.n
                P.wait("vector", sPE, pe_lg)
                P.op("vector", lambda e: e.reduce_max(af(F_ROWS, 1), PSF[:, 6 * 512:6 * 512 + 16],
                                                      AX.X, negate=True), inc=sDV)
                dv1 = sDV.n
                P.wait("scalar", sDV, dv1)
                P.op("scalar", lambda e: e.activation(af(F_ROWS + 32, 16), PSF[:, 6 * 512:6 * 512 + 16],
                                                      AF.Exp, bias=af(F_ROWS, 1), scale=1.0,
                                                      accum_out=af(F_ROWS + 1, 1)), inc=sAC)
                ac1 = sAC.n
                P.wait("vector", sAC, ac1)
                P.op("vector", lambda e: e.reciprocal(af(F_ROWS + 2, 1), af(F_ROWS + 1, 1)), inc=sDV)
                dv2 = sDV.n
                P.wait("vector", sDV, dv2)
                P.op("vector", lambda e: e.tensor_scalar(af(F_ROWS + 32, 16), af(F_ROWS + 32, 16),
                                                         af(F_ROWS + 2, 1), None, ALU.mult), inc=sDV)
                dv3 = sDV.n
                P.wait("vector", sDV, dv3)
                P.op("vector", lambda e: e.tensor_copy(af(F_ROWS + 48, 16), af(F_ROWS + 32, 16)), inc=sDV)
                for _r in range(TOPK):
                    P.wait("vector", sDV)
                    P.op("vector", lambda e: e.reduce_max(af(F_ROWS + 3, 1), af(F_ROWS + 48, 16),
                                                          AX.X), inc=sDV)
                    dvm = sDV.n
                    P.wait("vector", sDV, dvm)
                    P.op("vector", lambda e: e.tensor_scalar(af(F_ROWS + 64, 16), af(F_ROWS + 48, 16),
                                                             af(F_ROWS + 3, 1), None, ALU.is_equal), inc=sDV)
                    dve = sDV.n
                    P.wait("vector", sDV, dve)
                    P.op("vector", lambda e: e.scalar_tensor_tensor(
                        af(F_ROWS + 48, 16), af(F_ROWS + 64, 16), NEG, af(F_ROWS + 48, 16),
                        ALU.mult, ALU.add), inc=sDV)
                dvl = sDV.n
                P.wait("vector", sDV, dvl)
                P.op("vector", lambda e: e.tensor_tensor(af(F_ROWS + 64, 16), af(F_ROWS + 48, 16),
                                                         af(F_ROWS + 32, 16), ALU.not_equal), inc=sDV)
                dvs = sDV.n
                P.wait("vector", sDV, dvs)
                P.op("vector", lambda e, t=tc: e.tensor_tensor(cmb[:, t * 16:(t + 1) * 16],
                                                               af(F_ROWS + 64, 16),
                                                               af(F_ROWS + 32, 16), ALU.mult), inc=sDV)
            cmb_dv = sDV.n
            if tap2:
                P.wait("sync", sDV, cmb_dv)
                P.dma("sync", t_cmb[s, hf], cmb, sSt)

            P.wait("vector", sLd, seed_ld)
            w1_pe_done = [0, 0]
            w2_pe_done = [0, 0]
            down_prev_dv = [0, 0]
            gate_bank_dv = [0, 0]
            for ei, ex in enumerate(experts):
                act_off = C_ACT + (ei % 2) * (NPAIR * HALF)
                if ex == 16:
                    pairs_src = [("s", j) for j in range(11)]
                elif ex == 17:
                    pairs_src = [("s", j + 11) for j in range(11)]
                else:
                    pairs_src = [("e", j) for j in range(NPAIR)]
                for j, (src, jj) in enumerate(pairs_src):
                    wpar = j % 2
                    w1b = ab(C_W1 + wpar * 4096, 4096)
                    P.wait("gpsimd", sPE, w1_pe_done[wpar])
                    if src == "s":
                        P.dma("gpsimd", w1b, sw1p[jj], sW1)
                    else:
                        P.dma("gpsimd", w1b, w1p[ex, jj], sW1)
                    w1_ld = sW1.n
                    for tt in range(2):
                        gb = tt * 2
                        ub = tt * 2 + 1
                        P.wait("tensor", sW1, w1_ld)
                        P.wait("tensor", sDV, gate_bank_dv[tt])
                        for k in range(KD):
                            mm(bankf(gb), w1b[:, k * 256: k * 256 + 128],
                               x1t[:, k * HALF + tt * 512: k * HALF + (tt + 1) * 512],
                               k == 0, k == KD - 1, tag="gate")
                        for k in range(KD):
                            mm(bankf(ub), w1b[:, k * 256 + 128: k * 256 + 256],
                               x1t[:, k * HALF + tt * 512: k * HALF + (tt + 1) * 512],
                               k == 0, k == KD - 1, inc=sPE if k == KD - 1 else None, tag="gate")
                        pe_gu = sPE.n
                        if tt == 1:
                            w1_pe_done[wpar] = pe_gu
                        sg = ab(C_SG + tt * 512, 512)
                        P.wait("scalar", sPE, pe_gu)
                        P.op("scalar", lambda e, g=gb, s_=sg: e.activation(
                            s_, bankf(g), AF.Silu, bias=0.0, scale=1.0), inc=sAC)
                        ac_sg = sAC.n
                        P.wait("vector", sAC, ac_sg)
                        P.op("vector", lambda e, u=ub, s_=sg, o=act_off + j * 1024 + tt * 512:
                             e.tensor_mul(ab(o, 512), s_, bankf(u)), inc=sDV)
                        gate_bank_dv[tt] = sDV.n
                act_dv = sDV.n

                for hb in range(4):
                    wpar = hb % 2
                    w2b = ab(C_W2 + wpar * 5632, 5632)
                    P.wait("gpsimd", sPE, w2_pe_done[wpar])
                    if ex == 16:
                        P.dma("gpsimd", w2b, sw2p[0, hb], sW2)
                    elif ex == 17:
                        P.dma("gpsimd", w2b, sw2p[1, hb], sW2)
                    else:
                        P.dma("gpsimd", w2b, w2p[ex, hb], sW2)
                    w2_ld = sW2.n
                    for tc in range(8):
                        db = 4 + tc % 2
                        P.wait("tensor", sW2, w2_ld)
                        P.wait("tensor", sDV, act_dv)
                        P.wait("tensor", sDV, down_prev_dv[tc % 2])
                        for k in range(NPAIR):
                            mm(bankf(db), ab(act_off + k * 1024 + tc * 128, 128),
                               w2b[:, k * 512:(k + 1) * 512], k == 0, k == NPAIR - 1,
                               inc=sPE if k == NPAIR - 1 else None, tag="down")
                        pe_dn = sPE.n
                        if tc >= 6:
                            w2_pe_done[wpar] = pe_dn
                        P.wait("vector", sPE, pe_dn)
                        if ex >= 16:
                            P.op("vector", lambda e, b=db, t=tc, hb_=hb: e.scalar_tensor_tensor(
                                af(C_MOE + t * 2048 + hb_ * 512, 512), bankf(b), 1.0,
                                af(C_MOE + t * 2048 + hb_ * 512, 512), ALU.mult, ALU.add), inc=sDV)
                        else:
                            P.wait("vector", sDV, cmb_dv)
                            P.op("vector", lambda e, b=db, t=tc, hb_=hb, x_=ex: e.scalar_tensor_tensor(
                                af(C_MOE + t * 2048 + hb_ * 512, 512), bankf(b),
                                cmb[:, t * 16 + x_: t * 16 + x_ + 1],
                                af(C_MOE + t * 2048 + hb_ * 512, 512), ALU.mult, ALU.add), inc=sDV)
                        down_prev_dv[tc % 2] = sDV.n
            moe_dv = sDV.n
            if tap2:
                P.wait("sync", sDV, moe_dv)
                for tc in range(8):
                    P.dma("sync", t_h2[s, hf * HALF + tc * 128: hf * HALF + (tc + 1) * 128, :],
                          af(C_MOE + tc * 2048, 2048), sSt)
                ln2_extra = sSt.n
                P.wait("vector", sSt, ln2_extra)

            for tc in range(8):
                hv = af(C_MOE + tc * 2048, 2048)
                P.wait("vector", sDV, moe_dv)
                P.op("vector", lambda e, h_=hv: e.reduce_sum(af(F_ROWS + 4, 1), h_, AX.X), inc=sDV)
                dv_red = sDV.n
                P.wait("scalar", sDV, dv_red)
                P.op("scalar", lambda e, h_=hv: e.activation(
                    ab(C_DISC, 2048), h_, AF.Square, bias=0.0, scale=1.0,
                    accum_out=af(F_ROWS + 5, 1)), inc=sAC)
                P.wait("vector", sDV, dv_red)
                P.op("vector", lambda e: e.tensor_scalar(af(F_ROWS + 6, 1), af(F_ROWS + 4, 1),
                                                         1.0 / D, None, ALU.mult), inc=sDV)
                dv_m = sDV.n
                P.wait("scalar", sDV, dv_m)
                P.op("scalar", lambda e: e.activation(af(F_ROWS + 7, 1), af(F_ROWS + 6, 1),
                                                      AF.Square, bias=0.0, scale=1.0), inc=sAC)
                msq_ac = sAC.n
                P.wait("vector", sAC, msq_ac)
                P.op("vector", lambda e: e.scalar_tensor_tensor(
                    af(F_ROWS + 8, 1), af(F_ROWS + 5, 1), 1.0 / D, af(F_ROWS + 7, 1),
                    ALU.mult, ALU.subtract), inc=sDV)
                dv_var = sDV.n
                P.wait("scalar", sDV, dv_var)
                P.op("scalar", lambda e: e.activation(af(F_ROWS + 9, 1), af(F_ROWS + 8, 1),
                                                      AF.Ln, bias=ceps_ln[:, 0:1], scale=1.0), inc=sAC)
                P.op("scalar", lambda e: e.activation(af(F_ROWS + 9, 1), af(F_ROWS + 9, 1),
                                                      AF.Exp, bias=0.0, scale=-0.5), inc=sAC)
                rstd_ac = sAC.n
                outv = af(C_OUT, 2048)
                P.wait("vector", sAC, rstd_ac)
                P.wait("vector", sOut)
                P.op("vector", lambda e, h_=hv, o_=outv: e.tensor_scalar(
                    o_, h_, af(F_ROWS + 6, 1), af(F_ROWS + 9, 1), ALU.subtract, ALU.mult), inc=sDV)
                out_dv = sDV.n
                P.wait("sync", sDV, out_dv)
                P.dma("sync", y[s, hf * HALF + tc * 128: hf * HALF + (tc + 1) * 128, :], outv, sOut)

    P.wait("sync", sOut)

    with nc.Block() as block:
        @block.sync
        def _(e):
            for f in P.q["sync"]:
                f(e)

        @block.tensor
        def _(e):
            for f in P.q["tensor"]:
                f(e)

        @block.vector
        def _(e):
            for f in P.q["vector"]:
                f(e)

        @block.scalar
        def _(e):
            for f in P.q["scalar"]:
                f(e)

        @block.gpsimd
        def _(e):
            for f in P.q["gpsimd"]:
                f(e)

    return nc



def pack_weights(inp):
    f = np.ascontiguousarray
    Wq = np.asarray(inp["L0_attn_Wq"], dtype=np.float32)
    Wk = np.asarray(inp["L0_attn_Wk"], dtype=np.float32)
    Wv = np.asarray(inp["L0_attn_Wv"], dtype=np.float32)
    Wo = np.asarray(inp["L0_attn_Wo"], dtype=np.float32)
    w1 = np.asarray(inp["L0_ffn_w1"], dtype=np.float32)
    w2 = np.asarray(inp["L0_ffn_w2"], dtype=np.float32)
    sw1 = np.asarray(inp["L0_ffn_shared_w1"], dtype=np.float32)
    sw2 = np.asarray(inp["L0_ffn_shared_w2"], dtype=np.float32)
    gw = np.asarray(inp["L0_ffn_gate_w"], dtype=np.float32)

    WQP = f(Wq.reshape(KD, 128, H, 128).transpose(2, 1, 0, 3).reshape(H, 128, KD * 128))
    WKP = f(Wk.reshape(KD, 128, KVH, 128).transpose(2, 1, 0, 3).reshape(KVH, 128, KD * 128))
    WVP = f(Wv.reshape(KD, 128, KVH * DH))
    WOP = f(Wo.reshape(KD, 128, 4, 512).transpose(2, 1, 0, 3).reshape(4, 128, KD * 512))

    w1t = w1.transpose(0, 2, 1)
    g = w1t[:, :, :I_FF].reshape(E, KD, 128, NPAIR, 128)
    u = w1t[:, :, I_FF:].reshape(E, KD, 128, NPAIR, 128)
    gg = g.transpose(0, 3, 2, 1, 4)
    uu = u.transpose(0, 3, 2, 1, 4)
    W1P = f(np.stack([gg, uu], axis=4).reshape(E, NPAIR, 128, KD * 256))

    w2t = w2.transpose(0, 2, 1)
    W2P = f(w2t.reshape(E, NPAIR, 128, 4, 512).transpose(0, 3, 2, 1, 4).reshape(E, 4, 128, NPAIR * 512))

    sw1t = sw1.T
    gs = sw1t[:, :IS_FF].reshape(KD, 128, NPAIR_S, 128)
    us = sw1t[:, IS_FF:].reshape(KD, 128, NPAIR_S, 128)
    ggs = gs.transpose(2, 1, 0, 3)
    uus = us.transpose(2, 1, 0, 3)
    SW1P = f(np.stack([ggs, uus], axis=3).reshape(NPAIR_S, 128, KD * 256))

    sw2t = sw2.T
    SW2P = f(sw2t.reshape(2, NPAIR, 128, 4, 512).transpose(0, 3, 2, 1, 4).reshape(2, 4, 128, NPAIR * 512))

    GWP = f(gw.T.reshape(KD, 128, E).transpose(1, 0, 2).reshape(128, KD * E))
    QG = f(np.asarray(inp["L0_attn_q_gamma"], dtype=np.float32).reshape(128, 1))
    KG = f(np.asarray(inp["L0_attn_k_gamma"], dtype=np.float32).reshape(128, 1))
    return dict(wqp=WQP, wkp=WKP, wvp=WVP, wop=WOP, w1p=W1P, w2p=W2P,
                sw1p=SW1P, sw2p=SW2P, gwp=GWP, qg=QG, kg=KG)


def pack_x(x_core):
    f = np.ascontiguousarray
    x_tok = f(x_core.astype(np.float32, copy=False))
    xt = x_core.transpose(0, 2, 1)
    XTP = f(xt.reshape(SEQ_PER_CORE, KD, 128, NPANEL, PANEL)
            .transpose(0, 3, 2, 1, 4).reshape(SEQ_PER_CORE, NPANEL, 128, KD * PANEL))
    return x_tok, XTP


def kernel(**inputs):
    x_full = np.asarray(inputs["L0_attn_x"], dtype=np.float32)
    wmaps = pack_weights(inputs)
    nc = build_core()
    in_maps = []
    for ci in range(NCORES):
        x_tok, XTP = pack_x(x_full[2 * ci:2 * ci + 2])
        m = dict(wmaps)
        m["x"] = x_tok
        m["xtp"] = XTP
        in_maps.append(m)
    res = run_bass_kernel_spmd(nc, in_maps, core_ids=list(range(NCORES)))
    out = np.concatenate([r["y"] for r in res.results], axis=0)
    return out.reshape(B, S, D)
